# version 10 (final)
# baseline (speedup 1.0000x reference)
import math
import os
import contextlib
import numpy as np
import ml_dtypes

os.environ.setdefault("TRN_ALLOW_ICI_ALL_TO_ALL", "1")

import concourse.bass as bass
import concourse.mybir as mybir
from concourse.bass_utils import run_bass_kernel_spmd
from concourse.aot_env import allow_ici_all_to_all

F32 = mybir.dt.float32
BF16 = mybir.dt.bfloat16
AF = mybir.ActivationFunctionType
ALU = mybir.AluOpType
AX = mybir.AxisListType
BFNP = ml_dtypes.bfloat16

NCORE = 8
T = 8192
E = 2048
NH = 16
HD = 128
TC = T // NCORE
HPC = NH // NCORE
CH = E // 128
NCH = T // 128
H0 = 8192
H1 = 5632
RHE_C = 12582912.0
LN_EPS = 1e-5
RMS0_EPS = 1e-5
F32_EPS = float(np.finfo(np.float32).eps)
EXP_SCALE = 1.0 / math.sqrt(HD)
RG = [list(range(NCORE))]


class Op:
    __slots__ = ("eng", "fn", "deps", "sem", "inc", "cnt", "marked", "tag",
                 "self_inc")

    def __init__(self, eng, fn, deps, sem, inc, tag, self_inc=False):
        self.eng, self.fn, self.deps = eng, fn, [d for d in deps if d is not None]
        self.sem, self.inc = sem, inc
        self.cnt = None
        self.marked = False
        self.tag = tag
        self.self_inc = self_inc


class Builder:
    def __init__(self):
        self.ops = []

    def op(self, eng, fn, deps=(), tag="", inc=None, self_inc=False, sem=None):
        if sem is None:
            sem = {"T": "sT", "V": "sV", "S": "sS", "Y": "dY", "G": "dG",
                   "C": "sC"}[eng]
        if inc is None:
            inc = 16 if eng in ("Y", "G") else 1
        o = Op(eng, fn, deps, sem, inc, tag, self_inc)
        self.ops.append(o)
        return o

    def finalize(self):
        pos = {id(o): i for i, o in enumerate(self.ops)}
        for i, o in enumerate(self.ops):
            for d in o.deps:
                assert pos[id(d)] < i, f"forward dep {o.tag} -> {d.tag}"
                d.marked = True
        ctr = {}
        for o in self.ops:
            if o.eng in ("Y", "G"):
                o.marked = True
            if o.marked:
                ctr[o.sem] = ctr.get(o.sem, 0) + o.inc
                o.cnt = ctr[o.sem]

    def emit(self, blk, sems):
        streams = {"Y": [], "G": [], "T": [], "V": [], "S": []}
        for o in self.ops:
            streams["G" if o.eng == "C" else o.eng].append(o)

        def run(ename, e):
            waited = {}
            for o in streams[ename]:
                need = {}
                for d in o.deps:
                    if d.eng == o.eng and d.eng in ("T", "V", "S"):
                        continue
                    if d.cnt is None:
                        raise RuntimeError(f"{o.tag} dep on unmarked {d.tag}")
                    need[d.sem] = max(need.get(d.sem, 0), d.cnt)
                if o.eng in ("Y", "G") and o.inc:
                    prev = o.cnt - o.inc
                    if prev > 0:
                        need[o.sem] = max(need.get(o.sem, 0), prev)
                for sm, cnt in need.items():
                    if waited.get(sm, -1) < cnt:
                        e.wait_ge(sems[sm].h, cnt)
                        waited[sm] = cnt
                tr = _Track(e)
                o.fn(tr)
                if o.marked and not o.self_inc:
                    last = tr.last
                    if last is None:
                        last = e.engine_nop() if ename == "V" else e.nop()
                    last.then_inc(sems[o.sem].h, o.inc)

        @blk.sync
        def _(y):
            run("Y", y)

        @blk.gpsimd
        def _(g):
            run("G", g)

        @blk.tensor
        def _(t):
            run("T", t)

        @blk.vector
        def _(v):
            run("V", v)

        @blk.scalar
        def _(s):
            run("S", s)


class _Track:
    def __init__(self, e):
        self._e = e
        self.last = None

    def __getattr__(self, n):
        a = getattr(self._e, n)
        if callable(a):
            def w(*args, **kw):
                r = a(*args, **kw)
                if hasattr(r, "then_inc"):
                    self.last = r
                return r
            return w
        return a


class Sem:
    def __init__(self, h):
        self.h = h


def build_slopes(n):
    def pow2(n):
        start = 2.0 ** (-2.0 ** (-(math.log2(n) - 3)))
        return [start * start ** i for i in range(n)]
    return np.asarray(pow2(n), dtype=np.float64)


def build_program(stop_stage=None):
    nc = bass.Bass("TRN2", target_bir_lowering=False,
                   detect_race_conditions=False)
    B = Builder()
    OP = B.op
    stages = {}

    def ein(name, shape, dt=F32):
        return nc.dram_tensor(name, shape, dt, kind="ExternalInput").ap()

    xin = ein("xin", [E, TC])
    wq0s = ein("wq0s", [E, HPC * HD])
    wk0s = ein("wk0s", [E, HPC * HD])
    wv0s = ein("wv0s", [E, HPC * HD])
    wo0 = ein("wo0", [E, E])
    wq1s = ein("wq1s", [E, HPC * HD])
    wk1s = ein("wk1s", [E, HPC * HD])
    wv1s = ein("wv1s", [E, HPC * HD])
    wo1t = ein("wo1t", [E, E])
    rmspp = ein("rmspp", [128, CH])
    wmod0s = ein("wmod0s", [E, 1536])
    wmod1s = ein("wmod1s", [E, 1536])
    c0r = ein("c0r", [CH, 128])
    c1r = ein("c1r", [CH, 128])
    gup0s = ein("gup0s", [2 * H0 // NCORE, E])
    proj0s = ein("proj0s", [E // NCORE, H0])
    gup1s = ein("gup1s", [2 * H1 // NCORE, E])
    down1s = ein("down1s", [E // NCORE, H1])
    rotA = ein("rotA", [128, T])
    rotB = ein("rotB", [128, T])
    qdecin = ein("qdecin", [128, HPC])
    kdecin = ein("kdecin", [128, HPC])
    rcin = ein("rcin", [128, HPC])
    dmaskin = ein("dmaskin", [128, HPC * 128])
    causin = ein("causin", [128, 128])
    identfin = ein("identfin", [128, 128])
    identbin = ein("identbin", [128, 128], BF16)
    permin = ein("permin", [128, 128], BF16)

    xout = nc.dram_tensor("xout", [E, TC], F32, kind="ExternalOutput").ap()

    x_dram = nc.dram_tensor("x_dram", [E, TC], F32).ap()
    h0_scr = nc.dram_tensor("h0_scr", [H0, TC], BF16).ap()
    hq_scr = nc.dram_tensor("hq_scr", [H1, TC], BF16).ap()
    st_scr = nc.dram_tensor("st_scr", [2, 1024], F32).ap()

    mod_agi = nc.dram_tensor("mod_agi", [2, 1536], F32)
    mod_ago = nc.dram_tensor("mod_ago", [16, 1536], F32)
    gup0_agi = nc.dram_tensor("gup0_agi", [2 * H0 // NCORE, E], BF16)
    gup0_ag = nc.dram_tensor("gup0_ag", [2 * H0, E], BF16)
    proj0_agi = nc.dram_tensor("proj0_agi", [E // NCORE, H0], BF16)
    proj0_ag = nc.dram_tensor("proj0_ag", [E, H0], BF16)
    gup1_agi = nc.dram_tensor("gup1_agi", [2 * H1 // NCORE, E], BF16)
    gup1_ag = nc.dram_tensor("gup1_ag", [2 * H1, E], BF16)
    down1_agi = nc.dram_tensor("down1_agi", [E // NCORE, H1], BF16)
    down1_ag = nc.dram_tensor("down1_ag", [E, H1], BF16)
    hag_in = nc.dram_tensor("hag_in", [E, TC], BF16)
    hag_out = nc.dram_tensor("hag_out", [NCORE * E, TC], BF16)
    zaa_in = nc.dram_tensor("zaa_in", [T, HPC * HD], BF16)
    zaa_out = nc.dram_tensor("zaa_out", [T, HPC * HD], BF16)
    dbg1 = nc.dram_tensor("dbg1", [128, 256], F32, kind="ExternalOutput").ap()
    dbg2 = nc.dram_tensor("dbg2", [128, 128], BF16, kind="ExternalOutput").ap()
    dbg3 = nc.dram_tensor("dbg3", [128, 128], BF16, kind="ExternalOutput").ap()

    es = contextlib.ExitStack()
    sbt = lambda name, shape, dt=F32, **kw: es.enter_context(
        nc.sbuf_tensor(name, shape, dt, **kw))

    ident_f = sbt("ident_f", [128, 128])
    ident_b = sbt("ident_b", [128, 128], BF16)
    perm_b = sbt("perm_b", [128, 128], BF16)
    onesb = sbt("onesb", [128, 1], BF16)
    ones1f = sbt("ones1f", [1, 128])
    mod0 = sbt("mod0", [128, 96])
    mod1 = sbt("mod1", [128, 96])
    capp = sbt("capp", [128, CH])
    rms_pp = sbt("rms_pp", [128, CH])
    qdec_pp = sbt("qdec_pp", [128, HPC])
    kdec_pp = sbt("kdec_pp", [128, HPC])
    rc_pp = sbt("rc_pp", [128, HPC])
    dmask_s = sbt("dmask_s", [128, HPC * 128])
    caus_s = sbt("caus_s", [128, 128])
    st_a = sbt("st_a", [128, 64])
    st_b = sbt("st_b", [128, 64])
    st_c = sbt("st_c", [128, 64])
    st_bf = sbt("st_bf", [128, 64], BF16)

    AB_N = 68000
    AF_N = 10000
    arn_b = sbt("arn_b", [128, AB_N], BF16, is_transpose_dest=True)
    arn_f = sbt("arn_f", [128, AF_N])

    def vb(off, *shape):
        n = int(np.prod(shape))
        assert off + n <= AB_N, (off, shape)
        v = arn_b[:, off:off + n]
        if len(shape) > 1:
            dims = " ".join(f"d{i}" for i in range(len(shape)))
            v = v.rearrange(f"p ({dims}) -> p {dims}",
                            **{f"d{i}": s for i, s in enumerate(shape)})
        return v

    def vf(off, *shape):
        n = int(np.prod(shape))
        assert off + n <= AF_N, (off, shape)
        v = arn_f[:, off:off + n]
        if len(shape) > 1:
            dims = " ".join(f"d{i}" for i in range(len(shape)))
            v = v.rearrange(f"p ({dims}) -> p {dims}",
                            **{f"d{i}": s for i, s in enumerate(shape)})
        return v

    F_SIL = 0
    F_MST = 32
    F_XC = 300
    F_CINV = 2400
    F_CSC = 3424
    F_TMP = 4448
    F_ROT = 6500
    F_KV = 8560
    F_TI = 8816
    F_ROW = 8944
    F_WRK = 4448
    F_WRK = 300
    F_WMV = 6500

    A_H = 0
    A_QK = 16384
    A_VS = 49152
    A_BB = 65792
    hslab = vb(A_H, CH, TC)

    psA = es.enter_context(nc.psum_tensor("psA", [128, 1536], F32))
    psB = es.enter_context(nc.psum_tensor("psB", [128, 1536], F32))
    psO1 = es.enter_context(nc.psum_tensor("psO1", [128, 512], F32))
    psO2 = es.enter_context(nc.psum_tensor("psO2", [128, 512], F32))

    sems = {}

    def _alloc_sems():
        names = sorted({o.sem for o in B.ops})
        for k in names:
            sems[k] = Sem(es.enter_context(nc.semaphore(k)))

    def DMA(eng, out, in_, deps=(), tag="dma", transpose=False, slot=None):
        def fn(e, out=out, in_=in_, transpose=transpose):
            return e.dma_start(out=out, in_=in_, transpose=transpose)
        return OP(eng, fn, deps, tag=tag, sem=slot)

    d_const = []
    for dst, src in ((ident_f[:], identfin), (ident_b[:], identbin),
                     (perm_b[:], permin), (rms_pp[:], rmspp),
                     (qdec_pp[:], qdecin), (kdec_pp[:], kdecin),
                     (rc_pp[:], rcin), (dmask_s[:], dmaskin),
                     (caus_s[:], causin)):
        d_const.append(DMA("Y", dst, src, tag="const", slot="cst"))
    d_x0 = DMA("Y", x_dram, xin, tag="xcopy", slot="x0")
    o_ones = OP("V", lambda v: (v.memset(onesb[:], 1.0),
                                v.memset(ones1f[:], 1.0))[-1],
                deps=d_const, tag="ones")

    csil = vf(F_SIL, 32)
    mstg = vf(F_MST, 2, 128)
    rowAll = vf(F_XC, 2, 1536)
    wmv = [vf(F_WMV, 2, 1536)[:, i, :] for i in range(2)]

    d_c0 = DMA("Y", mstg[0:CH, 0, :], c0r, tag="c0", slot="cst")
    d_c1 = DMA("Y", mstg[0:CH, 1, :], c1r, tag="c1", slot="cst")
    o_ctr = OP("T", lambda t: (t.transpose(psA[:, 0:16], mstg[0:CH, 0, :], ident_f[0:CH, 0:CH]),
                               t.transpose(psA[:, 16:32], mstg[0:CH, 1, :], ident_f[0:CH, 0:CH]))[-1],
               deps=[d_c0, d_c1, o_ones], tag="ctr")
    o_sil = OP("S", lambda s: (s.activation(out=csil[:, 0:16], in_=psA[:, 0:16], func=AF.Silu),
                               s.activation(out=csil[:, 16:32], in_=psA[:, 16:32], func=AF.Silu))[-1],
               deps=[o_ctr], tag="sil")

    prev_mm = {}
    ev_ops = []
    for li in range(2):
        src = wmod0s if li == 0 else wmod1s
        for k in range(CH):
            idx = li * CH + k
            d = DMA("Y", wmv[idx % 2], src[128 * k:128 * k + 128, :],
                    deps=[prev_mm.get(idx - 2),
                          ev_ops[0] if (li == 1 and k < 2) else None],
                    tag=f"wm{idx}", slot=f"wm{idx % 2}")

            def mmfn(t, li=li, k=k, idx=idx):
                for nb in range(3):
                    t.matmul(psB[0:1, 512 * nb:512 * nb + 512],
                             csil[:, 16 * li + k:16 * li + k + 1],
                             wmv[idx % 2][:, 512 * nb:512 * nb + 512],
                             start=(k == 0), stop=(k == CH - 1))
                return None
            m = OP("T", mmfn, deps=[d, o_sil,
                                    ev_ops[0] if (li == 1 and k == 0) else None],
                   tag=f"wmm{idx}")
            prev_mm[idx] = m
        ev = OP("V", (lambda v, li=li: v.tensor_copy(
            out=rowAll[0:1, li, :], in_=psB[0:1, 0:1536])),
            deps=[prev_mm[li * CH + CH - 1]], tag=f"wmev{li}")
        ev_ops.append(ev)

    d_magi0 = DMA("G", mod_agi.ap()[0:1, :], rowAll[0:1, 0, :], deps=[ev_ops[0]], tag="mg_i0", slot="mgi")
    d_magi1 = DMA("G", mod_agi.ap()[1:2, :], rowAll[0:1, 1, :], deps=[ev_ops[1]], tag="mg_i1", slot="mgi")
    o_magc = OP("C", lambda g: g.collective_compute(
        "AllGather", ALU.bypass, replica_groups=RG,
        ins=[mod_agi.ap()], outs=[mod_ago.ap()]),
        deps=[d_magi0, d_magi1], tag="modag")
    mago = mod_ago.ap().rearrange("(r l) d -> r l d", l=2)
    def mgfn(g):
        for li in range(2):
            for r in range(NCORE):
                g.dma_start(
                    out=mstg[12 * r:12 * r + 12, li, :],
                    in_=mago[r, li, :].rearrange("(l d) -> l d", d=128)
                ).then_inc(sems["mgo"].h, 16)
        return None
    d_mg0 = OP("G", mgfn, deps=[o_magc], tag="mg0", inc=16 * 16, self_inc=True, sem="mgo")
    d_mg1 = d_mg0
    o_mtr = OP("T", lambda t: (t.transpose(psA[:, 0:96], mstg[0:96, 0, :], ident_f[0:96, 0:96]),
                               t.transpose(psA[:, 96:192], mstg[0:96, 1, :], ident_f[0:96, 0:96]))[-1],
               deps=[d_mg0, d_mg1], tag="mtr")

    o_modv1 = OP("V", (lambda v: (v.tensor_copy(out=mod0[:], in_=psA[:, 0:96]),
                                  v.tensor_copy(out=mod1[:], in_=psA[:, 96:192]))[-1]),
                 deps=[o_mtr], tag="modv1")

    def modsgn(s):
        for mi, m in enumerate((mod0, mod1)):
            for ci_, cc in enumerate((32, 80)):
                col = 16 * (2 * mi + ci_)
                s.activation(out=st_a[:, col:col + 16], in_=m[:, cc:cc + 16],
                             func=AF.Sign)
                s.activation(out=st_b[:, col:col + 16], in_=m[:, cc:cc + 16],
                             func=AF.Abs)
        return None
    o_mods = OP("S", modsgn, deps=[o_modv1], tag="modsgn")

    def modfin(v):
        for mi, m in enumerate((mod0, mod1)):
            for ci_, cc in enumerate((32, 80)):
                col = 16 * (2 * mi + ci_)
                v.tensor_scalar(out=st_b[:, col:col + 16], in0=st_b[:, col:col + 16],
                                scalar1=0.25, scalar2=0.25, op0=ALU.max, op1=ALU.bypass)
        for mi, m in enumerate((mod0, mod1)):
            for ci_, cc in enumerate((32, 80)):
                col = 16 * (2 * mi + ci_)
                v.tensor_tensor(out=m[:, cc:cc + 16], in0=st_a[:, col:col + 16],
                                in1=st_b[:, col:col + 16], op=ALU.mult)
        return None
    o_mod = OP("V", modfin, deps=[o_mods], tag="modfin")
    if os.environ.get("KMK_DBG") == "mod":
        DMA("G", dbg1[:, 0:96], mod0[:, 0:96], deps=[o_mod], tag="dbgm0",
            slot="xs0")
        DMA("G", dbg1[:, 96:112], st_a[:, 0:16], deps=[o_mod], tag="dbgsa",
            slot="xs1")
        DMA("G", dbg1[:, 112:128], st_b[:, 0:16], deps=[o_mod], tag="dbgsb",
            slot="xl0")
        DMA("G", dbg1[:, 128:224], mod1[:, 0:96], deps=[o_mod], tag="dbgm1",
            slot="xl1")
    stages["mod"] = len(B.ops)

    wt_in = vb(A_QK, 8192)
    wq_out = vb(A_QK + 8192, 8192)
    wf_r = vf(F_WRK, 5632)

    def tern_fn(v, ncols):
        ng = ncols // 64
        src = wt_in[:, 0:ncols]
        dst = wq_out[:, 0:ncols]
        s3 = src.rearrange("p (g k) -> p g k", k=64)
        v.tensor_reduce(out=st_a[:, 0:ng].rearrange("p g -> p g ()"), in_=s3,
                        axis=AX.X, op=ALU.add, apply_absolute_value=True)
        v.tensor_scalar(out=st_a[:, 0:ng], in0=st_a[:, 0:ng],
                        scalar1=1.0 / 64, scalar2=1e-8, op0=ALU.mult, op1=ALU.max)
        v.tensor_copy(out=st_bf[:, 0:ng], in_=st_a[:, 0:ng])
        v.tensor_copy(out=st_c[:, 0:ng], in_=st_bf[:, 0:ng])
        v.reciprocal(out=st_b[:, 0:ng], in_=st_c[:, 0:ng])
        w3 = wf_r[:, 0:ncols].rearrange("p (g k) -> p g k", k=64)
        b0, b1 = bass.broadcast_tensor_aps(
            s3, st_b[:, 0:ng].rearrange("p g -> p g ()"))
        v.tensor_tensor(out=w3, in0=b0, in1=b1, op=ALU.mult)
        fl = wf_r[:, 0:ncols]
        v.tensor_scalar(out=fl, in0=fl, scalar1=RHE_C, scalar2=RHE_C,
                        op0=ALU.add, op1=ALU.subtract)
        v.tensor_scalar(out=fl, in0=fl, scalar1=1.0, scalar2=-1.0,
                        op0=ALU.min, op1=ALU.max)
        b0, b1 = bass.broadcast_tensor_aps(
            w3, st_c[:, 0:ng].rearrange("p g -> p g ()"))
        v.tensor_tensor(out=dst.rearrange("p (g k) -> p g k", k=64),
                        in0=b0, in1=b1, op=ALU.mult)
        return None

    def int8_fn(v, ncols):
        src = wt_in[:, 0:ncols]
        dst = wq_out[:, 0:ncols]
        v.tensor_reduce(out=st_a[:, 0:1].rearrange("p g -> p g ()"),
                        in_=src.rearrange("p n -> p () n"),
                        axis=AX.X, op=ALU.max, apply_absolute_value=True)
        v.tensor_scalar(out=st_a[:, 0:1], in0=st_a[:, 0:1],
                        scalar1=1.0 / 127.0, scalar2=1e-8, op0=ALU.mult, op1=ALU.max)
        v.reciprocal(out=st_b[:, 0:1], in_=st_a[:, 0:1])
        fl = wf_r[:, 0:ncols]
        v.tensor_scalar(out=fl, in0=src, scalar1=st_b[:, 0:1],
                        scalar2=st_b[:, 0:1], op0=ALU.mult, op1=ALU.bypass)
        v.tensor_scalar(out=fl, in0=fl, scalar1=RHE_C, scalar2=RHE_C,
                        op0=ALU.add, op1=ALU.subtract)
        v.tensor_scalar(out=dst, in0=fl, scalar1=st_a[:, 0:1],
                        scalar2=st_a[:, 0:1], op0=ALU.mult, op1=ALU.bypass)
        return None

    pre_jobs = []
    for i in range(16):
        pre_jobs.append(("tern", gup0s[128 * i:128 * i + 128, :],
                         gup0_agi.ap()[128 * i:128 * i + 128, :], E))
    for i in range(2):
        for j in range(2):
            pre_jobs.append(("tern",
                             proj0s[128 * i:128 * i + 128, 4096 * j:4096 * j + 4096],
                             proj0_agi.ap()[128 * i:128 * i + 128, 4096 * j:4096 * j + 4096],
                             4096))
    for i in range(11):
        pre_jobs.append(("int8", gup1s[128 * i:128 * i + 128, :],
                         gup1_agi.ap()[128 * i:128 * i + 128, :], E))
    for i in range(2):
        pre_jobs.append(("int8", down1s[128 * i:128 * i + 128, :],
                         down1_agi.ap()[128 * i:128 * i + 128, :], H1))

    def tern_steps(ncols):
        ng = ncols // 64
        src = wt_in[:, 0:ncols]
        dst = wq_out[:, 0:ncols]
        s3 = src.rearrange("p (g k) -> p g k", k=64)
        w3 = wf_r[:, 0:ncols].rearrange("p (g k) -> p g k", k=64)
        fl = wf_r[:, 0:ncols]

        def s_red(v):
            v.tensor_reduce(out=st_a[:, 0:ng].rearrange("p g -> p g ()"),
                            in_=s3, axis=AX.X, op=ALU.add,
                            apply_absolute_value=True)
        def s_scale(v):
            v.tensor_scalar(out=st_a[:, 0:ng], in0=st_a[:, 0:ng],
                            scalar1=1.0 / 64, scalar2=1e-8,
                            op0=ALU.mult, op1=ALU.max)
        def s_bf(v):
            v.tensor_copy(out=st_bf[:, 0:ng], in_=st_a[:, 0:ng])
        def s_c(v):
            v.tensor_copy(out=st_c[:, 0:ng], in_=st_bf[:, 0:ng])
        def s_rcp(v):
            v.reciprocal(out=st_b[:, 0:ng], in_=st_c[:, 0:ng])
        def s_div(v):
            b0, b1 = bass.broadcast_tensor_aps(
                s3, st_b[:, 0:ng].rearrange("p g -> p g ()"))
            v.tensor_tensor(out=w3, in0=b0, in1=b1, op=ALU.mult)
        def s_rhe(v):
            v.tensor_scalar(out=fl, in0=fl, scalar1=RHE_C, scalar2=RHE_C,
                            op0=ALU.add, op1=ALU.subtract)
        def s_clip(v):
            v.tensor_scalar(out=fl, in0=fl, scalar1=1.0, scalar2=-1.0,
                            op0=ALU.min, op1=ALU.max)
        def s_out(v):
            b0, b1 = bass.broadcast_tensor_aps(
                w3, st_c[:, 0:ng].rearrange("p g -> p g ()"))
            v.tensor_tensor(out=dst.rearrange("p (g k) -> p g k", k=64),
                            in0=b0, in1=b1, op=ALU.mult)
        return [s_red, s_scale, s_bf, s_c, s_rcp, s_div, s_rhe, s_clip, s_out]

    def int8_steps(ncols):
        src = wt_in[:, 0:ncols]
        dst = wq_out[:, 0:ncols]
        fl = wf_r[:, 0:ncols]

        def s_red(v):
            v.tensor_reduce(out=st_a[:, 0:1].rearrange("p g -> p g ()"),
                            in_=src.rearrange("p n -> p () n"),
                            axis=AX.X, op=ALU.max, apply_absolute_value=True)
        def s_scale(v):
            v.tensor_scalar(out=st_a[:, 0:1], in0=st_a[:, 0:1],
                            scalar1=1.0 / 127.0, scalar2=1e-8,
                            op0=ALU.mult, op1=ALU.max)
        def s_rcp(v):
            v.reciprocal(out=st_b[:, 0:1], in_=st_a[:, 0:1])
        def s_div(v):
            v.tensor_scalar(out=fl, in0=src, scalar1=st_b[:, 0:1],
                            scalar2=st_b[:, 0:1], op0=ALU.mult, op1=ALU.bypass)
        def s_rhe(v):
            v.tensor_scalar(out=fl, in0=fl, scalar1=RHE_C, scalar2=RHE_C,
                            op0=ALU.add, op1=ALU.subtract)
        def s_out(v):
            v.tensor_scalar(out=dst, in0=fl, scalar1=st_a[:, 0:1],
                            scalar2=st_a[:, 0:1], op0=ALU.mult, op1=ALU.bypass)
        return [s_red, s_scale, s_rcp, s_div, s_rhe, s_out]

    prev_q = {}
    prev_sv = {}
    for j, (kind, src, dst, ncols) in enumerate(pre_jobs):
        d = DMA("G", wt_in[:, 0:ncols], src,
                deps=[prev_q.get(j - 1), o_mod if j == 0 else None],
                tag=f"pl{j}", slot="pl")
        steps = tern_steps(ncols) if kind == "tern" else int8_steps(ncols)
        prev = None
        for si, fnv in enumerate(steps):
            dep0 = [d, prev_sv.get(j - 1)] if si == 0 else [prev]
            qv = OP("V", (lambda v, fnv=fnv: fnv(v)), deps=dep0,
                    tag=f"pq{j}_{si}")
            if si < len(steps) - 1:
                prev = OP("S", (lambda s: None), deps=[qv], tag=f"pb{j}_{si}")
            else:
                prev = qv
        q = prev
        sv = DMA("G", dst, wq_out[:, 0:ncols], deps=[q], tag=f"psv{j}", slot="ps")
        prev_q[j] = q
        prev_sv[j] = sv
    pre_last_store = prev_sv[len(pre_jobs) - 1]

    ag_ops = []
    prev_c = None
    for nmi, (i_, o_) in enumerate(((gup0_agi, gup0_ag), (proj0_agi, proj0_ag),
                                    (gup1_agi, gup1_ag), (down1_agi, down1_ag))):
        cop = OP("C", (lambda g, i_=i_, o_=o_: g.collective_compute(
            "AllGather", ALU.bypass, replica_groups=RG,
            ins=[i_.ap()], outs=[o_.ap()])),
            deps=[pre_last_store, prev_c or o_magc], tag=f"wag{nmi}")
        ag_ops.append(cop)
        prev_c = cop
    ag_gup0, ag_proj0, ag_gup1, ag_down1 = ag_ops
    pre_done = ag_down1
    stages["pre"] = len(B.ops)

    xc_v = [vf(F_XC, 2, 1024)[:, i, :] for i in range(2)]
    cinv_v = vf(F_CINV, 1024)
    cs_v = vf(F_CSC, 1024)
    tmp_v = [vf(F_TMP, 2, 1024)[:, i, :] for i in range(2)]
    rowst = vf(F_ROW, 1024)
    xb_t = [vb(A_BB, 2, 1024)[:, i, :] for i in range(2)]
    xb_t = [vb(A_QK + 24576 + 2048 * i, 2048) for i in range(2)]

    def row_h(base, n):
        return rowst[0:1, base:base + n] if False else vf(F_ROW, 1024)[0:1, base:base + n]

    def emit_ln(mtile, col_s, col_c, deps_in, tag):
        o_cap = OP("V", (lambda v: v.tensor_scalar(
            out=capp[:], in0=mtile[:, col_c:col_c + 16], scalar1=1.0, scalar2=0.0,
            op0=ALU.add, op1=ALU.bypass)), deps=deps_in + [o_mod], tag=tag + "cap")
        prev_t = {}
        for ch in range(CH):
            d = DMA("Y", xc_v[ch % 2], x_dram[128 * ch:128 * ch + 128, :],
                    deps=deps_in + [prev_t.get(ch - 2)], tag=f"{tag}x{ch}",
                    slot=f"lnx{ch % 2}")

            def vfn(v, ch=ch):
                v.tensor_copy(out=xb_t[ch % 2][:, 0:1024], in_=xc_v[ch % 2])
                v.tensor_tensor(out=xb_t[ch % 2][:, 1024:2048],
                                in0=xb_t[ch % 2][:, 0:1024],
                                in1=xb_t[ch % 2][:, 0:1024], op=ALU.mult)
                return None
            vo = OP("V", vfn, deps=[d, prev_t.get(ch - 2)], tag=f"{tag}v{ch}")

            def tfn(t, ch=ch):
                for nb in range(2):
                    t.matmul((psO1 if nb == 0 else psO2)[0:1, 0:512],
                             onesb[:], xb_t[ch % 2][:, 512 * nb:512 * nb + 512],
                             start=(ch == 0), stop=(ch == CH - 1))
                    t.matmul(psA[0:1, 512 * nb:512 * nb + 512],
                             onesb[:], xb_t[ch % 2][:, 1024 + 512 * nb:1536 + 512 * nb],
                             start=(ch == 0), stop=(ch == CH - 1))
                return None
            to = OP("T", tfn, deps=[vo], tag=f"{tag}t{ch}")
            prev_t[ch] = to
        stat_mm = prev_t[CH - 1]

        def finfn(v):
            mr = vf(F_TMP, 1024)[0:1, :]
            vr = vf(F_ROW, 1024)[0:1, :]
            t2 = vf(F_TMP + 1024, 1024)[0:1, :]
            v.tensor_scalar(out=mr[0:1, 0:512], in0=psO1[0:1, 0:512],
                            scalar1=1.0 / E, scalar2=0.0, op0=ALU.mult, op1=ALU.bypass)
            v.tensor_scalar(out=mr[0:1, 512:1024], in0=psO2[0:1, 0:512],
                            scalar1=1.0 / E, scalar2=0.0, op0=ALU.mult, op1=ALU.bypass)
            v.tensor_scalar(out=vr, in0=psA[0:1, 0:1024],
                            scalar1=1.0 / E, scalar2=0.0, op0=ALU.mult, op1=ALU.bypass)
            v.tensor_tensor(out=t2, in0=mr, in1=mr, op=ALU.mult)
            v.tensor_tensor(out=vr, in0=vr, in1=t2, op=ALU.subtract)
            v.tensor_scalar(out=vr, in0=vr, scalar1=LN_EPS, scalar2=0.0,
                            op0=ALU.add, op1=ALU.bypass)
            return None
        o_fin = OP("V", finfn, deps=[stat_mm], tag=tag + "fin")
        o_sq = OP("S", (lambda s: s.activation(out=vf(F_TMP + 1024, 1024)[0:1, :],
                                               in_=vf(F_ROW, 1024)[0:1, :], func=AF.Sqrt)),
                  deps=[o_fin], tag=tag + "sqrt")

        def newtfn(v):
            mr = vf(F_TMP, 1024)[0:1, :]
            vr = vf(F_ROW, 1024)[0:1, :]
            sr = vf(F_TMP + 1024, 1024)[0:1, :]
            t2 = vf(F_CINV, 1024)[0:1, :]
            v.reciprocal(out=t2, in_=sr)
            v.tensor_tensor(out=t2, in0=vr, in1=t2, op=ALU.mult)
            v.tensor_tensor(out=sr, in0=sr, in1=t2, op=ALU.add)
            v.tensor_scalar(out=sr, in0=sr, scalar1=0.5, scalar2=0.0,
                            op0=ALU.mult, op1=ALU.bypass)
            v.reciprocal(out=vr, in_=sr)
            v.tensor_tensor(out=mr, in0=mr, in1=vr, op=ALU.mult)
            return None
        o_newt = OP("V", newtfn, deps=[o_sq], tag=tag + "newt")

        def bcfn(t):
            vr = vf(F_ROW, 1024)[0:1, :]
            mr = vf(F_TMP, 1024)[0:1, :]
            for half in range(2):
                t.matmul(psB[:, 512 * half:512 * half + 512], ones1f[:],
                         vr[0:1, 512 * half:512 * half + 512], start=True, stop=True)
                t.matmul((psO1 if half == 0 else psO2)[:, 0:512], ones1f[:],
                         mr[0:1, 512 * half:512 * half + 512], start=True, stop=True)
            return None
        o_bc = OP("T", bcfn, deps=[o_newt, stat_mm], tag=tag + "bc")

        def bevfn(v):
            v.tensor_copy(out=cinv_v[:, 0:512], in_=psB[:, 0:512])
            v.tensor_copy(out=cinv_v[:, 512:1024], in_=psB[:, 512:1024])
            v.tensor_copy(out=cs_v[:, 0:512], in_=psO1[:, 0:512])
            v.tensor_copy(out=cs_v[:, 512:1024], in_=psO2[:, 0:512])
            return None
        o_bev = OP("V", bevfn, deps=[o_bc], tag=tag + "bev")

        last_ap = None
        prev_ap = {}
        for ch in range(CH):
            d = DMA("Y", xc_v[ch % 2], x_dram[128 * ch:128 * ch + 128, :],
                    deps=[o_bev, prev_ap.get(ch - 2)], tag=f"{tag}y{ch}",
                    slot=f"lnx{ch % 2}")

            def apfn(v, ch=ch):
                v.tensor_tensor(out=tmp_v[ch % 2], in0=xc_v[ch % 2],
                                in1=cinv_v[:], op=ALU.mult)
                v.tensor_tensor(out=tmp_v[ch % 2], in0=tmp_v[ch % 2],
                                in1=cs_v[:], op=ALU.subtract)
                v.tensor_scalar(out=hslab[:, ch, :], in0=tmp_v[ch % 2],
                                scalar1=capp[:, ch:ch + 1],
                                scalar2=mtile[:, col_s + ch:col_s + ch + 1],
                                op0=ALU.mult, op1=ALU.add)
                return None
            a2 = OP("V", apfn, deps=[d, o_bev, o_cap], tag=f"{tag}a2_{ch}")
            prev_ap[ch] = a2
            last_ap = a2
        return last_ap

    def emit_residual(groups, gate_tile, gate_col, use_cinv, deps_mm, tag,
                      to_xout=False):
        outs = []
        for gi, (p0, p1, mbg) in enumerate(groups):
            d = DMA("G", xc_v[gi % 2], x_dram[128 * mbg:128 * mbg + 128, :],
                    deps=deps_mm + [outs[gi - 2] if gi >= 2 else None],
                    tag=f"{tag}xl{gi}", slot=f"xl{gi % 2}")

            def vfn(v, p0=p0, p1=p1, gi=gi):
                if use_cinv:
                    v.tensor_tensor(out=tmp_v[gi % 2][:, 0:512], in0=p0,
                                    in1=cinv_v[:, 0:512], op=ALU.mult)
                    v.tensor_tensor(out=tmp_v[gi % 2][:, 512:1024], in0=p1,
                                    in1=cinv_v[:, 512:1024], op=ALU.mult)
                else:
                    v.tensor_copy(out=tmp_v[gi % 2][:, 0:512], in_=p0)
                    v.tensor_copy(out=tmp_v[gi % 2][:, 512:1024], in_=p1)
                return None
            o1 = OP("V", vfn, deps=deps_mm, tag=f"{tag}v1_{gi}")
            o2 = OP("S", (lambda s, gi=gi, mbg=mbg: s.activation(
                out=tmp_v[gi % 2], in_=tmp_v[gi % 2], func=AF.Copy,
                scale=gate_tile[:, gate_col + mbg:gate_col + mbg + 1])),
                deps=[o1], tag=f"{tag}s_{gi}")
            o3 = OP("V", (lambda v, gi=gi: v.tensor_tensor(
                out=xc_v[gi % 2], in0=xc_v[gi % 2], in1=tmp_v[gi % 2], op=ALU.add)),
                deps=[o2, d], tag=f"{tag}v2_{gi}")
            dst = xout if to_xout else x_dram
            st = DMA("G", dst[128 * mbg:128 * mbg + 128, :], xc_v[gi % 2],
                     deps=[o3], tag=f"{tag}xs{gi}", slot=f"xs{gi % 2}")
            outs.append(st)
        return outs

    def emit_bcast_from_stats(cols_a, cols_b, deps_in, tag):
        o_tr = OP("T", (lambda t: (t.transpose(psA[0:8, 0:128], st_b[:, 0:8], ident_f[:]),
                                   t.transpose(psA[0:8, 128:256], st_a[:, 0:8], ident_f[:]))[-1]),
                  deps=deps_in, tag=tag + "tr")
        o_te = OP("V", (lambda v: (v.tensor_copy(out=st_c[0:8, 0:32].rearrange("p n -> p n") if False else vf(F_MST, 2, 128)[0:8, 0, :], in_=psA[0:8, 0:128]),
                                   v.tensor_copy(out=vf(F_MST, 2, 128)[0:8, 1, :], in_=psA[0:8, 128:256]))[-1]),
                  deps=[o_tr], tag=tag + "te")
        d1 = DMA("G", st_scr[0:1, :].rearrange("o (b n) -> (o b) n", n=128),
                 vf(F_MST, 2, 128)[0:8, 0, :], deps=[o_te], tag=tag + "d1", slot="ssA")
        d2 = DMA("G", st_scr[1:2, :].rearrange("o (b n) -> (o b) n", n=128),
                 vf(F_MST, 2, 128)[0:8, 1, :], deps=[o_te], tag=tag + "d2", slot="ssA")
        d3 = DMA("G", vf(F_TMP, 1024)[0:1, :], st_scr[0:1, :], deps=[d1, d2],
                 tag=tag + "d3", slot="ssB")
        d4 = DMA("G", vf(F_TMP + 1024, 1024)[0:1, :], st_scr[1:2, :], deps=[d1, d2],
                 tag=tag + "d4", slot="ssB")

        def bcfn(t):
            for half in range(2):
                t.matmul(psB[:, 512 * half:512 * half + 512], ones1f[:],
                         vf(F_TMP, 1024)[0:1, 512 * half:512 * half + 512],
                         start=True, stop=True)
                t.matmul((psO1 if half == 0 else psO2)[:, 0:512], ones1f[:],
                         vf(F_TMP + 1024, 1024)[0:1, 512 * half:512 * half + 512],
                         start=True, stop=True)
            return None
        o_bc = OP("T", bcfn, deps=[d3, d4], tag=tag + "bc")
        o_be = OP("V", (lambda v: (v.tensor_copy(out=cinv_v[:, 0:512], in_=psB[:, 0:512]),
                                   v.tensor_copy(out=cinv_v[:, 512:1024], in_=psB[:, 512:1024]),
                                   v.tensor_copy(out=cs_v[:, 0:512], in_=psO1[:, 0:512]),
                                   v.tensor_copy(out=cs_v[:, 512:1024], in_=psO2[:, 0:512]))[-1]),
                  deps=[o_bc], tag=tag + "be")
        return o_be

    ln1_last = emit_ln(mod0, 0, 16, [d_x0, o_mod, pre_done], tag="ln1")
    stages["ln1"] = len(B.ops)
    d_hag0 = DMA("G", hag_in.ap().rearrange("(c p) t -> p c t", p=128), hslab[:, :, :],
                 deps=[ln1_last], tag="hag_in0", slot="hg")
    c_hag0 = OP("C", lambda g: g.collective_compute(
        "AllGather", ALU.bypass, replica_groups=RG,
        ins=[hag_in.ap()], outs=[hag_out.ap()]), deps=[d_hag0], tag="hag0")
    stages["hag0"] = len(B.ops)

    wqv = {"q": vb(A_H, CH, 256), "k": vb(A_H + 4096, CH, 256),
           "v": vb(A_H + 8192, CH, 256)}
    d_wq0 = {}
    for qn, src in (("q", wq0s), ("k", wk0s), ("v", wv0s)):
        d_wq0[qn] = DMA("G", wqv[qn][:, :, :], src.rearrange("(c p) i -> p c i", p=128),
                        deps=[ln1_last, d_hag0], tag=f"w0{qn}", slot=f"wl{qn}")
    stages["w0l"] = len(B.ops)

    bb = [vb(A_QK + 8192 * i, CH, 512) for i in range(2)]
    qt_t = [vb(A_QK + 16384 + 1024 * i, 2, 512) for i in range(2)]
    kt_t = [vb(A_QK + 18432 + 1024 * i, 2, 512) for i in range(2)]
    raw_t = vb(A_QK + 20480, 512)
    vt_t = [vb(A_VS + 1100 * i, 4, 2, 129) for i in range(2)]
    lt_sc = vb(A_VS + 2200, 128)
    lt_ki = vb(A_VS + 2328, 128)
    lt_kb = [vb(A_VS + 2456 + 128 * i, 128) for i in range(2)]
    zb_t = [vb(A_VS + 2712 + 256 * i, 2, 128) for i in range(2)]
    rotAv = [vf(F_ROT + 512 * i, 512) for i in range(2)]
    rotBv = [vf(F_ROT + 1024 + 512 * i, 512) for i in range(2)]
    kv_v = [vf(F_KV + 128 * i, 128) for i in range(2)]
    ti_v = vf(F_TI, 128)

    o_kv0 = OP("V", lambda v: (v.memset(kv_v[0][:], 0.0),
                               v.memset(kv_v[1][:], 0.0))[-1],
               deps=[], tag="kv0")

    QKPS = [psA[:, 0:512], psA[:, 512:1024], psA[:, 1024:1536], psB[:, 0:512]]
    VPS = [psB[:, 512:768], psB[:, 1024:1280], psO1[:, 0:256], psO2[:, 0:256]]
    SWPS = psO1[:, 0:512]
    LTPS = [psO2[:, 0:512], psO1[:, 0:512]]

    prev_qkro = {}
    prev_vev = {}
    prev_lt = {}
    prev_zst = {}
    for tb in range(16):
        dhb = DMA("Y", bb[tb % 2],
                  hag_out.ap()[2048 * (tb // 2):2048 * (tb // 2) + 2048,
                               512 * (tb % 2):512 * (tb % 2) + 512]
                  .rearrange("(c p) n -> p c n", p=128),
                  deps=[c_hag0, prev_qkro.get(tb - 2), prev_vev.get((tb - 2, 3))],
                  tag=f"hb{tb}", slot=f"hb{tb % 2}")
        drA = DMA("Y", rotAv[tb % 2], rotA[:, 512 * tb:512 * tb + 512],
                  deps=[prev_qkro.get(tb - 2)], tag=f"rA{tb}", slot=f"rA{tb % 2}")
        drB = DMA("Y", rotBv[tb % 2], rotB[:, 512 * tb:512 * tb + 512],
                  deps=[prev_qkro.get(tb - 2)], tag=f"rB{tb}", slot=f"rB{tb % 2}")
        stages[f"hbl{tb}"] = len(B.ops)

        qk_ro = []
        for ci_, (qk, h) in enumerate((("q", 0), ("q", 1), ("k", 0), ("k", 1))):
            def mmfn(t, qk=qk, h=h, ci_=ci_, tb=tb):
                for k in range(CH):
                    t.matmul(QKPS[ci_], wqv[qk][:, k, 128 * h:128 * h + 128],
                             bb[tb % 2][:, k, :], start=(k == 0), stop=(k == CH - 1))
                return None
            mm = OP("T", mmfn, deps=[dhb, d_wq0[qk], prev_qkro.get(tb - 1)],
                    tag=f"qk{tb}_{ci_}")
            ev = OP("V", (lambda v, ci_=ci_: v.tensor_copy(out=raw_t[:], in_=QKPS[ci_])),
                    deps=[mm, prev_qkro.get((tb, ci_ - 1)) if False else
                          (qk_ro[-1] if qk_ro else None)], tag=f"qkev{tb}_{ci_}")
            sw = OP("T", (lambda t: t.matmul(SWPS, perm_b[:], raw_t[:],
                                             start=True, stop=True)),
                    deps=[ev, prev_lt.get((tb - 1, 3))], tag=f"sw{tb}_{ci_}")

            def rotfn(v, qk=qk, h=h, tb=tb):
                dstt = (qt_t if qk == "q" else kt_t)[tb % 2][:, h, :]
                v.tensor_tensor(out=tmp_v[0][:, 0:512], in0=raw_t[:],
                                in1=rotAv[tb % 2][:], op=ALU.mult)
                v.tensor_tensor(out=tmp_v[1][:, 0:512], in0=SWPS,
                                in1=rotBv[tb % 2][:], op=ALU.mult)
                v.tensor_tensor(out=dstt, in0=tmp_v[0][:, 0:512],
                                in1=tmp_v[1][:, 0:512], op=ALU.add)
                return None
            ro = OP("V", rotfn, deps=[sw, drA, drB, prev_lt.get((tb - 2, 3))],
                    tag=f"rot{tb}_{ci_}")
            qk_ro.append(ro)
            stages[f"qr{tb}_{ci_}"] = len(B.ops)
        prev_qkro[tb] = qk_ro[-1]

        vevs = []
        for j in range(4):
            def vmfn(t, j=j, tb=tb):
                for k in range(CH):
                    t.matmul(VPS[j], bb[tb % 2][:, k, 128 * j:128 * j + 128],
                             wqv["v"][:, k, :], start=(k == 0), stop=(k == CH - 1))
                return None
            vm = OP("T", vmfn, deps=[dhb, d_wq0["v"], prev_vev.get((tb - 1, j)),
                                     qk_ro[3] if j >= 2 else None,
                                     prev_lt.get((tb - 1, 3)) if j >= 2 else None],
                    tag=f"vm{tb}_{j}")
            def vevfn(v, j=j, tb=tb):
                v.tensor_copy(out=vt_t[tb % 2][:, j, 0, 0:128], in_=VPS[j][:, 0:128])
                v.tensor_copy(out=vt_t[tb % 2][:, j, 1, 0:128], in_=VPS[j][:, 128:256])
                return None
            vev = OP("V", vevfn,
                deps=[vm, prev_lt.get((tb - 2, j))], tag=f"vev{tb}_{j}")
            prev_vev[(tb, j)] = vev
            vevs.append(vev)
        stages[f"qkv{tb}"] = len(B.ops)

        for j in range(4):
            ci = 4 * tb + j
            par = ci % 2
            bank = LTPS[par]
            sc_ps = bank[:, 0:128]
            tr_ps = bank[:, 128:192].bitcast(BF16)
            in_ps = bank[:, 256:384]
            it_ps = bank[:, 384:512]
            kv_ps = bank[:, 0:128]
            prev_par = prev_lt.get((tb, j - 2)) or prev_lt.get((tb - 1, j + 2))
            for h in range(2):
                hguard = prev_lt.get((tb, j)) if h == 1 else prev_lt.get((tb, j - 1))

                def scfn(t, j=j, tb=tb, h=h, sc_ps=sc_ps, tr_ps=tr_ps):
                    t.matmul(sc_ps, kt_t[tb % 2][:, h, 128 * j:128 * j + 128],
                             qt_t[tb % 2][:, h, 128 * j:128 * j + 128],
                             start=True, stop=True)
                    t.transpose(tr_ps, kt_t[tb % 2][:, h, 128 * j:128 * j + 128],
                                ident_b[:])
                    return None
                tsc = OP("T", scfn, deps=[qk_ro[h], qk_ro[2 + h], prev_par, hguard,
                                          vevs[3]],
                         tag=f"lsc{ci}_{h}")
                tki = OP("S", (lambda s, h=h, tr_ps=tr_ps: s.activation(
                    out=lt_ki[:], in_=tr_ps, func=AF.Copy,
                    scale=kdec_pp[:, h:h + 1])), deps=[tsc], tag=f"lki{ci}_{h}")
                def scmfn(v, h=h, sc_ps=sc_ps):
                    v.tensor_tensor(out=lt_sc[:], in0=sc_ps,
                                    in1=dmask_s[:, 128 * h:128 * h + 128], op=ALU.mult)
                    v.tensor_copy(out=lt_kb[h][:], in_=kv_v[h][:])
                    return None
                tscm = OP("V", scmfn, deps=[tsc, tki, o_kv0, hguard],
                          tag=f"lscm{ci}_{h}")

                def immfn(t, j=j, tb=tb, h=h, in_ps=in_ps, it_ps=it_ps, kv_ps=kv_ps):
                    t.matmul(in_ps, qt_t[tb % 2][:, h, 128 * j:128 * j + 128],
                             lt_kb[h][:], start=True, stop=True)
                    t.matmul(it_ps, lt_sc[:], vt_t[tb % 2][:, j, h, 0:128],
                             start=True, stop=True)
                    t.matmul(kv_ps, lt_ki[:], vt_t[tb % 2][:, j, h, 0:128],
                             start=True, stop=True)
                    return None
                tim = OP("T", immfn, deps=[tki, tscm, vevs[j]], tag=f"lmm{ci}_{h}")
                tzi = OP("S", (lambda s, h=h, in_ps=in_ps: s.activation(
                    out=ti_v[:], in_=in_ps, func=AF.Copy,
                    scale=qdec_pp[:, h:h + 1])), deps=[tim], tag=f"lzi{ci}_{h}")

                def zefn(v, h=h, it_ps=it_ps, kv_ps=kv_ps, par=par):
                    v.tensor_tensor(out=zb_t[par][:, h, :], in0=ti_v[:], in1=it_ps,
                                    op=ALU.add)
                    v.tensor_scalar(out=kv_v[h][:], in0=kv_v[h][:],
                                    scalar1=rc_pp[:, h:h + 1], scalar2=rc_pp[:, h:h + 1],
                                    op0=ALU.mult, op1=ALU.bypass)
                    v.tensor_tensor(out=kv_v[h][:], in0=kv_v[h][:], in1=kv_ps,
                                    op=ALU.add)
                    return None
                tze = OP("V", zefn, deps=[tzi, tim, prev_zst.get(ci - 2)],
                         tag=f"lze{ci}_{h}")
                prev_lt[(tb, j)] = tze
            zst = DMA("G", zaa_in.ap()[128 * ci:128 * ci + 128, :],
                      zb_t[par].rearrange("p h n -> p (h n)"),
                      deps=[prev_lt[(tb, j)]], tag=f"zst{ci}", slot=f"zs{par}")
            prev_zst[ci] = zst
        stages[f"tb{tb}"] = len(B.ops)

    stages["l0tb"] = len(B.ops)
    c_zaa0 = OP("C", lambda g: g.collective_compute(
        "AllToAll", ALU.bypass, replica_groups=RG,
        ins=[zaa_in.ap()], outs=[zaa_out.ap()]),
        deps=[prev_zst[NCH - 2], prev_zst[NCH - 1]], tag="zaa0")

    def emit_zproj(c_z, wsrc, apply_rms, gate_tile, gate_col, tag, to_xout=False):
        ztok = vb(A_QK, 8, 2048)
        zTw = vb(A_QK + 16384, CH, 1024)
        wtl = [vb(A_VS + 512 * i, 512) for i in range(2)]
        sqt = vb(A_VS + 1024, 2048)
        zlds = []
        for r in range(NCORE):
            zlds.append(DMA("Y", ztok[:, :, 256 * r:256 * r + 256],
                            zaa_out.ap()[1024 * r:1024 * r + 1024, :]
                            .rearrange("(b p) n -> p b n", p=128),
                            deps=[c_z], tag=f"{tag}zl{r}", slot="zl"))
        if apply_rms:
            sts = []
            for b in range(8):
                sts.append(OP("S", (lambda s, b=b: s.activation(
                    out=sqt[:], in_=ztok[:, b, :], func=AF.Square,
                    accum_out=st_a[:, b:b + 1])), deps=zlds, tag=f"{tag}sq{b}"))

            def invfn(v):
                v.tensor_scalar(out=st_a[:, 0:8], in0=st_a[:, 0:8], scalar1=1.0 / E,
                                scalar2=RMS0_EPS, op0=ALU.mult, op1=ALU.add)
                return None
            o_iv = OP("V", invfn, deps=sts, tag=tag + "iv")
            o_sq2 = OP("S", (lambda s: s.activation(out=st_b[:, 0:8], in_=st_a[:, 0:8],
                                                    func=AF.Sqrt)),
                       deps=[o_iv], tag=tag + "sq2")

            def newt2(v):
                v.reciprocal(out=st_c[:, 0:8], in_=st_b[:, 0:8])
                v.tensor_tensor(out=st_c[:, 0:8], in0=st_a[:, 0:8], in1=st_c[:, 0:8],
                                op=ALU.mult)
                v.tensor_tensor(out=st_b[:, 0:8], in0=st_b[:, 0:8], in1=st_c[:, 0:8],
                                op=ALU.add)
                v.tensor_scalar(out=st_b[:, 0:8], in0=st_b[:, 0:8], scalar1=0.5,
                                scalar2=0.0, op0=ALU.mult, op1=ALU.bypass)
                v.reciprocal(out=st_b[:, 0:8], in_=st_b[:, 0:8])
                return None
            o_nw = OP("V", newt2, deps=[o_sq2], tag=tag + "nw")
            o_be = emit_bcast_from_stats(None, None, [o_nw, stat_free(tag)], tag + "cb")
            stats_done = [o_be]
        else:
            stats_done = zlds

        tr_ops = []
        for b in range(8):
            for fc4 in range(4):
                def trfn(t, b=b, fc4=fc4):
                    for q in range(4):
                        fc = 4 * fc4 + q
                        t.transpose(psA[:, 512 * (fc4 % 2) + 64 * q:512 * (fc4 % 2) + 64 * q + 64]
                                    .bitcast(BF16),
                                    ztok[:, b, 128 * fc:128 * fc + 128], ident_b[:])
                    return None
                deps_tr = list(stats_done) + \
                    [tr_ops[-2]["ev"] if len(tr_ops) >= 2 else None]
                o_tr = OP("T", trfn, deps=deps_tr, tag=f"{tag}tr{b}_{fc4}")

                def evfn(s, b=b, fc4=fc4):
                    for q in range(4):
                        fc = 4 * fc4 + q
                        kw = dict(scale=rms_pp[:, fc:fc + 1]) if apply_rms else {}
                        s.activation(out=zTw[:, fc, 128 * b:128 * b + 128],
                                     in_=psA[:, 512 * (fc4 % 2) + 64 * q:512 * (fc4 % 2) + 64 * q + 64]
                                     .bitcast(BF16), func=AF.Copy, **kw)
                    return None
                o_ev = OP("S", evfn, deps=[o_tr], tag=f"{tag}te{b}_{fc4}")
                tr_ops.append({"tr": o_tr, "ev": o_ev})
        tr_last = tr_ops[-1]["ev"]

        ACC = [psA[:, 0:512], psA[:, 512:1024], psA[:, 1024:1536],
               psB[:, 0:512], psB[:, 512:1024], psB[:, 1024:1536],
               psO1[:, 0:512], psO2[:, 0:512]]
        res_sts = []
        prev_res = None
        for p4 in range(4):
            mm_last = None
            mms = {}
            for k in range(CH):
                dwl = DMA("G", wtl[k % 2][:], wsrc[128 * k:128 * k + 128,
                                                   512 * p4:512 * p4 + 512],
                          deps=[tr_last, prev_res, mms.get(k - 2)],
                          tag=f"{tag}wl{p4}_{k}", slot=f"wt{k % 2}")

                def mmfn(t, k=k):
                    for mb in range(4):
                        for nb in range(2):
                            t.matmul(ACC[2 * mb + nb],
                                     wtl[k % 2][:, 128 * mb:128 * mb + 128],
                                     zTw[:, k, 512 * nb:512 * nb + 512],
                                     start=(k == 0), stop=(k == CH - 1))
                    return None
                mm_last = OP("T", mmfn, deps=[dwl, tr_last, prev_res],
                             tag=f"{tag}mm{p4}_{k}")
                mms[k] = mm_last
            groups = [(ACC[2 * mb], ACC[2 * mb + 1], 4 * p4 + mb) for mb in range(4)]
            sts = emit_residual(groups, gate_tile, gate_col, apply_rms,
                                [mm_last], f"{tag}r{p4}", to_xout=to_xout)
            res_sts.extend(sts)
            prev_res = sts[-1]
        return res_sts

    def stat_free(tag):
        return None

    stages["l0attn"] = len(B.ops)
    res1 = emit_zproj(c_zaa0, wo0, True, mod0, 32, "w0_")
    stages["l0wo"] = len(B.ops)

    stages["ln2"] = len(B.ops)
    ln2_last = emit_ln(mod0, 48, 64, [res1[-1]], tag="ln2")

    gulT = [vb(A_QK + 16384 * i, CH, 2, 512) for i in range(2)]
    sg_t = vb(A_VS, 512)
    hh_t = vb(A_VS + 512, 512)
    sq_t = vb(A_VS + 1024, 512)
    GPS = {0: psA[:, 0:512], 1: psA[:, 512:1024]}
    UPS = {0: psA[:, 1024:1536], 1: psB[:, 0:512]}
    S2PS = [psB[0:1, 512:1024], psB[0:1, 1024:1536]]

    prev_gu = {}
    prev_hst = {}
    for grp in range(16):
        def wldfn(y, grp=grp):
            for k in range(CH):
                y.dma_start(out=gulT[grp % 2][:, k, 0, :],
                            in_=gup0_ag.ap()[512 * grp:512 * grp + 512,
                                             128 * k:128 * k + 128],
                            transpose=True).then_inc(sems[f"gw{grp % 2}"].h, 16)
                y.dma_start(out=gulT[grp % 2][:, k, 1, :],
                            in_=gup0_ag.ap()[H0 + 512 * grp:H0 + 512 * grp + 512,
                                             128 * k:128 * k + 128],
                            transpose=True).then_inc(sems[f"gw{grp % 2}"].h, 16)
            return None
        dwl = OP("Y", wldfn, deps=[ag_gup0, ln2_last, prev_gu.get(8 * (grp - 2) + 7)],
                 tag=f"guw{grp}", inc=16 * 32, self_inc=True, sem=f"gw{grp % 2}")
        for i4 in range(4):
            strip = 4 * grp + i4
            for half in range(2):
                par = (2 * i4 + half) % 2
                unit = 8 * grp + 2 * i4 + half

                def mmfn(t, grp=grp, i4=i4, half=half, par=par):
                    for k in range(CH):
                        t.matmul(GPS[par], gulT[grp % 2][:, k, 0, 128 * i4:128 * i4 + 128],
                                 hslab[:, k, 512 * half:512 * half + 512],
                                 start=(k == 0), stop=(k == CH - 1))
                        t.matmul(UPS[par], gulT[grp % 2][:, k, 1, 128 * i4:128 * i4 + 128],
                                 hslab[:, k, 512 * half:512 * half + 512],
                                 start=(k == 0), stop=(k == CH - 1))
                    return None
                mm = OP("T", mmfn, deps=[dwl, ln2_last, prev_gu.get(unit - 2)],
                        tag=f"gum{strip}_{half}")
                sil = OP("S", (lambda s, par=par: s.activation(
                    out=sg_t[:], in_=GPS[par], func=AF.Silu)), deps=[mm],
                    tag=f"gus{strip}_{half}")

                def hmfn(v, par=par):
                    v.tensor_tensor(out=hh_t[:], in0=sg_t[:], in1=UPS[par], op=ALU.mult)
                    v.tensor_tensor(out=sq_t[:], in0=hh_t[:], in1=hh_t[:], op=ALU.mult)
                    return None
                hm = OP("V", hmfn, deps=[sil, mm, prev_hst.get(unit - 2)],
                        tag=f"guh{strip}_{half}")
                stm = OP("T", (lambda t, half=half, strip=strip: t.matmul(
                    S2PS[half], onesb[:], sq_t[:],
                    start=(strip == 0), stop=(strip == 63))),
                    deps=[hm], tag=f"gut{strip}_{half}")
                hst = DMA("G", h0_scr[128 * strip:128 * strip + 128,
                                      512 * half:512 * half + 512], hh_t[:],
                          deps=[hm], tag=f"guq{strip}_{half}", slot=f"hs{half}")
                prev_gu[unit] = stm
                prev_hst[unit] = hst
    gu_stat_last = prev_gu[127]
    gu_hst_last = prev_hst[127]

    def ffnfin(v):
        vr = vf(F_ROW, 1024)[0:1, :]
        v.tensor_scalar(out=vr[0:1, 0:512], in0=S2PS[0], scalar1=1.0 / H0,
                        scalar2=F32_EPS, op0=ALU.mult, op1=ALU.add)
        v.tensor_scalar(out=vr[0:1, 512:1024], in0=S2PS[1], scalar1=1.0 / H0,
                        scalar2=F32_EPS, op0=ALU.mult, op1=ALU.add)
        return None
    o_ff = OP("V", ffnfin, deps=[gu_stat_last], tag="f0fin")
    o_fs = OP("S", (lambda s: s.activation(out=vf(F_TMP, 1024)[0:1, :],
                                           in_=vf(F_ROW, 1024)[0:1, :], func=AF.Sqrt)),
              deps=[o_ff], tag="f0sqrt")

    def ffnnewt(v):
        vr = vf(F_ROW, 1024)[0:1, :]
        sr = vf(F_TMP, 1024)[0:1, :]
        t2 = vf(F_TMP + 1024, 1024)[0:1, :]
        v.reciprocal(out=t2, in_=sr)
        v.tensor_tensor(out=t2, in0=vr, in1=t2, op=ALU.mult)
        v.tensor_tensor(out=sr, in0=sr, in1=t2, op=ALU.add)
        v.tensor_scalar(out=sr, in0=sr, scalar1=0.5, scalar2=0.0,
                        op0=ALU.mult, op1=ALU.bypass)
        v.reciprocal(out=sr, in_=sr)
        return None
    o_fn2 = OP("V", ffnnewt, deps=[o_fs], tag="f0newt")
    o_fbc = OP("T", (lambda t: (t.matmul(psO1[:, 0:512], ones1f[:],
                                         vf(F_TMP, 1024)[0:1, 0:512], start=True, stop=True),
                                t.matmul(psO2[:, 0:512], ones1f[:],
                                         vf(F_TMP, 1024)[0:1, 512:1024], start=True, stop=True))[-1]),
               deps=[o_fn2, gu_stat_last], tag="f0bc")
    o_fbe = OP("V", (lambda v: (v.tensor_copy(out=cinv_v[:, 0:512], in_=psO1[:, 0:512]),
                                v.tensor_copy(out=cinv_v[:, 512:1024], in_=psO2[:, 0:512]))[-1]),
              deps=[o_fbc], tag="f0be")

    ACC = [psA[:, 0:512], psA[:, 512:1024], psA[:, 1024:1536],
           psB[:, 0:512], psB[:, 512:1024], psB[:, 1024:1536],
           psO1[:, 0:512], psO2[:, 0:512]]
    pj_w = [vb(A_VS + 512 * i, 512) for i in range(2)]
    pj_h = [vb(A_VS + 1024 + 1024 * i, 1024) for i in range(2)]
    res2 = []
    prev_res = None
    gu_hst_l2 = prev_hst[126]
    for p4 in range(4):
        mm_last = None
        mms = {}
        for k in range(64):
            dwl = DMA("Y", pj_w[k % 2][:],
                      proj0_ag.ap()[512 * p4:512 * p4 + 512, 128 * k:128 * k + 128],
                      deps=[ag_proj0, o_fbe, prev_res, gu_hst_last, gu_hst_l2,
                            mms.get(k - 2)],
                      tag=f"pjw{p4}_{k}", transpose=True, slot=f"pw{k % 2}")
            dhl = DMA("Y", pj_h[k % 2][:], h0_scr[128 * k:128 * k + 128, :],
                      deps=[gu_hst_last, gu_hst_l2, prev_res, mms.get(k - 2)],
                      tag=f"pjh{p4}_{k}", slot=f"ph{k % 2}")

            def mmfn(t, k=k):
                for mb in range(4):
                    for nb in range(2):
                        t.matmul(ACC[2 * mb + nb], pj_w[k % 2][:, 128 * mb:128 * mb + 128],
                                 pj_h[k % 2][:, 512 * nb:512 * nb + 512],
                                 start=(k == 0), stop=(k == 63))
                return None
            mm_last = OP("T", mmfn, deps=[dwl, dhl, prev_res, o_fbe], tag=f"pjm{p4}_{k}")
            mms[k] = mm_last
        groups = [(ACC[2 * mb], ACC[2 * mb + 1], 4 * p4 + mb) for mb in range(4)]
        sts = emit_residual(groups, mod0, 80, True, [mm_last], f"pjr{p4}")
        res2.extend(sts)
        prev_res = sts[-1]

    stages["l0ffn"] = len(B.ops)
    ln3_last = emit_ln(mod1, 0, 16, [res2[-1]], tag="ln3")
    d_hag1 = DMA("G", hag_in.ap().rearrange("(c p) t -> p c t", p=128), hslab[:, :, :],
                 deps=[ln3_last], tag="hag_in1", slot="hg")
    c_hag1 = OP("C", lambda g: g.collective_compute(
        "AllGather", ALU.bypass, replica_groups=RG,
        ins=[hag_in.ap()], outs=[hag_out.ap()]), deps=[d_hag1], tag="hag1")

    wqv1 = {"q": vb(A_H, CH, 256), "k": vb(A_H + 4096, CH, 256),
            "v": vb(A_H + 8192, CH, 256)}
    d_wq1 = {}
    for qn, src in (("q", wq1s), ("k", wk1s), ("v", wv1s)):
        d_wq1[qn] = DMA("G", wqv1[qn][:, :, :], src.rearrange("(c p) i -> p c i", p=128),
                        deps=[ln3_last, d_hag1], tag=f"w1{qn}", slot=f"wl{qn}")

    kT1 = [vb(A_QK + 8192 * h, T) for h in range(2)]
    q1_t = vb(A_QK + 16384, 2, 512)
    bb1 = vb(A_QK + 17408, CH, 512)
    vsl1 = vb(A_VS, NCH, 2, 129)
    expb = [vb(A_BB + 0, 1536) if False else vb(A_QK + 25600 + 1536 * i, 1536)
            for i in range(2)]
    zb1 = vb(A_QK + 28672, 4, 256)

    o_vone = OP("V", (lambda v: v.memset(vsl1[:, :, :, 128:129], 1.0)),
                deps=[res2[-1]], tag="vone")

    QKPS1 = [psA[:, 0:512], psA[:, 512:1024], psA[:, 1024:1536], psB[:, 0:512]]
    VPS1 = [psB[:, 512:768], psB[:, 1024:1280], psO1[:, 0:256], psO2[:, 0:256]]
    SC_G = [psA[:, 0:1536], psB[:, 0:1536]]
    OPS1 = [psO1, psO2]

    prev_vev1 = None
    prev_att = None
    prev_zst1 = None
    for tb in range(16):
        dhb = DMA("Y", bb1,
                  hag_out.ap()[2048 * (tb // 2):2048 * (tb // 2) + 2048,
                               512 * (tb % 2):512 * (tb % 2) + 512]
                  .rearrange("(c p) n -> p c n", p=128),
                  deps=[c_hag1, prev_vev1], tag=f"h1b{tb}", slot="hb0")
        qk1_ev = []
        for ci_, (qk, h) in enumerate((("q", 0), ("q", 1), ("k", 0), ("k", 1))):
            def mmfn(t, qk=qk, h=h, ci_=ci_, tb=tb):
                for k in range(CH):
                    t.matmul(QKPS1[ci_], wqv1[qk][:, k, 128 * h:128 * h + 128],
                             bb1[:, k, :], start=(k == 0), stop=(k == CH - 1))
                return None
            mm = OP("T", mmfn, deps=[dhb, d_wq1[qk], prev_att], tag=f"q1m{tb}_{ci_}")

            def evfn(v, qk=qk, h=h, ci_=ci_, tb=tb):
                if qk == "q":
                    v.tensor_copy(out=q1_t[:, h, :], in_=QKPS1[ci_])
                else:
                    v.tensor_copy(out=kT1[h][:, 512 * tb:512 * tb + 512],
                                  in_=QKPS1[ci_])
                return None
            ev = OP("V", evfn, deps=[mm], tag=f"q1e{tb}_{ci_}")
            qk1_ev.append(ev)
        for j in range(4):
            def vmfn(t, j=j, tb=tb):
                for k in range(CH):
                    t.matmul(VPS1[j], bb1[:, k, 128 * j:128 * j + 128],
                             wqv1["v"][:, k, :], start=(k == 0), stop=(k == CH - 1))
                return None
            vm = OP("T", vmfn, deps=[dhb, d_wq1["v"], prev_att], tag=f"v1m{tb}_{j}")
            def vevfn1(v, j=j, tb=tb):
                v.tensor_copy(out=vsl1[:, 4 * tb + j, 0, 0:128], in_=VPS1[j][:, 0:128])
                v.tensor_copy(out=vsl1[:, 4 * tb + j, 1, 0:128], in_=VPS1[j][:, 128:256])
                return None
            vev = OP("V", vevfn1,
                deps=[vm, o_vone], tag=f"v1e{tb}_{j}")
            prev_vev1 = vev

        nk = 4 * tb + 4
        zevs = []
        for h in range(2):
            kgroups = [list(range(g, min(g + 3, nk))) for g in range(0, nk, 3)]
            pv_ops = []
            prev_exp = {}
            for gi, ks in enumerate(kgroups):
                par = gi % 2

                def smmfn(t, ks=ks, par=par, h=h, tb=tb):
                    for ii, kk in enumerate(ks):
                        t.matmul(SC_G[par][:, 512 * ii:512 * ii + 512],
                                 kT1[h][:, 128 * kk:128 * kk + 128],
                                 q1_t[:, h, :], start=True, stop=True)
                    return None
                smm = OP("T", smmfn,
                         deps=[qk1_ev[h], qk1_ev[2 + h], prev_exp.get(gi - 2),
                               prev_vev1 if gi == 0 else None],
                         tag=f"s1{tb}_{h}_{gi}")
                exp = OP("S", (lambda s, ks=ks, par=par: s.activation(
                    out=expb[par][:, 0:512 * len(ks)],
                    in_=SC_G[par][:, 0:512 * len(ks)], func=AF.Exp,
                    scale=EXP_SCALE)),
                    deps=[smm, pv_ops[-2] if len(pv_ops) >= 2 else None],
                    tag=f"e1{tb}_{h}_{gi}")
                prev_exp[gi] = exp
                diag = [kk for kk in ks if kk >= 4 * tb]
                dm = exp
                if diag:
                    def dmfn(v, ks=ks, diag=diag, tb=tb, par=par):
                        for kk in diag:
                            ii = ks.index(kk)
                            j = kk - 4 * tb
                            sl = expb[par][:, 512 * ii + 128 * j:512 * ii + 128 * j + 128]
                            v.tensor_tensor(out=sl, in0=sl, in1=caus_s[:], op=ALU.mult)
                        return None
                    dm = OP("V", dmfn, deps=[exp], tag=f"d1{tb}_{h}_{gi}")

                def pvfn(t, ks=ks, tb=tb, h=h, par=par):
                    for ii, kk in enumerate(ks):
                        for j in range(4):
                            if kk <= 4 * tb + j:
                                base = 129 * (j % 2)
                                t.matmul(OPS1[j // 2][:, base:base + 129],
                                         expb[par][:, 512 * ii + 128 * j:512 * ii + 128 * j + 128],
                                         vsl1[:, kk, h, 0:129],
                                         start=(kk == 0), stop=(kk == 4 * tb + j))
                    return None
                pv = OP("T", pvfn, deps=[dm, prev_zst1 if gi == 0 and h == 0 else None,
                                         zevs[-1] if (gi == 0 and h == 1) else None],
                        tag=f"p1{tb}_{h}_{gi}")
                pv_ops.append(pv)

            dn1 = OP("V", (lambda v: (
                [v.tensor_copy(out=st_a[:, j:j + 1],
                               in_=OPS1[j // 2][:, 129 * (j % 2) + 128:129 * (j % 2) + 129])
                 for j in range(4)],
                None)[-1]), deps=[pv_ops[-1]], tag=f"n1a{tb}_{h}")
            dnb = OP("S", (lambda s: None), deps=[dn1], tag=f"n1b{tb}_{h}")
            dn = OP("V", (lambda v: v.reciprocal(out=st_b[:, 0:4],
                                                 in_=st_a[:, 0:4])),
                    deps=[dnb], tag=f"n1{tb}_{h}")

            def zevfn(s, tb=tb, h=h):
                for j in range(4):
                    base = 129 * (j % 2)
                    s.activation(out=zb1[:, j, 128 * h:128 * h + 128],
                                 in_=OPS1[j // 2][:, base:base + 128], func=AF.Copy,
                                 scale=st_b[:, j:j + 1])
                return None
            ze = OP("S", zevfn, deps=[dn, prev_zst1], tag=f"z1{tb}_{h}")
            zevs.append(ze)
            prev_att = ze
        zst = DMA("G", zaa_in.ap()[512 * tb:512 * tb + 512, :]
                  .rearrange("(j p) n -> p j n", p=128), zb1[:, :, :],
                  deps=zevs, tag=f"zs1{tb}", slot="zs0")
        prev_zst1 = zst

    c_zaa1 = OP("C", lambda g: g.collective_compute(
        "AllToAll", ALU.bypass, replica_groups=RG,
        ins=[zaa_in.ap()], outs=[zaa_out.ap()]),
        deps=[prev_zst1], tag="zaa1")

    stages["l1attn"] = len(B.ops)
    res3 = emit_zproj(c_zaa1, wo1t, False, mod1, 32, "w1_")
    stages["l1wo"] = len(B.ops)

    ln4_last = emit_ln(mod1, 48, 64, [res3[-1]], tag="ln4")

    tt_t = vb(A_QK, 2048)
    prev_amax = None
    for b in range(8):
        evs = []
        for fc4 in range(4):
            def trfn(t, b=b, fc4=fc4):
                for q in range(4):
                    fc = 4 * fc4 + q
                    t.transpose(psA[:, 512 * (fc4 % 2) + 64 * q:512 * (fc4 % 2) + 64 * q + 64]
                                .bitcast(BF16),
                                hslab[:, fc, 128 * b:128 * b + 128], ident_b[:])
                return None
            o_tr = OP("T", trfn, deps=[ln4_last,
                                       evs[-2] if len(evs) >= 2 else prev_amax],
                      tag=f"xt{b}_{fc4}")
            o_ev = OP("V", (lambda v, fc4=fc4: v.tensor_copy(
                out=tt_t[:, 512 * fc4:512 * fc4 + 512],
                in_=psA[:, 512 * (fc4 % 2):512 * (fc4 % 2) + 256].bitcast(BF16))),
                deps=[o_tr], tag=f"xe{b}_{fc4}")
            evs.append(o_ev)

        def amfn(v, b=b):
            v.tensor_reduce(out=st_a[:, b:b + 1].rearrange("p g -> p g ()"),
                            in_=tt_t[:, :].rearrange("p n -> p () n"),
                            axis=AX.X, op=ALU.max, apply_absolute_value=True)
            return None
        b_ev = OP("S", (lambda s: None), deps=[evs[-1]], tag=f"xab{b}")
        prev_amax = OP("V", amfn, deps=[b_ev] + evs, tag=f"xa{b}")

    b_am = OP("S", (lambda s: None), deps=[prev_amax], tag="xqb0")
    o_xq1 = OP("V", (lambda v: v.tensor_scalar(
        out=st_a[:, 0:8], in0=st_a[:, 0:8], scalar1=1.0 / 127.0,
        scalar2=1e-8, op0=ALU.mult, op1=ALU.max)), deps=[b_am], tag="xqs1")
    b_xq = OP("S", (lambda s: None), deps=[o_xq1], tag="xqb1")
    o_xqs = OP("V", (lambda v: v.reciprocal(out=st_b[:, 0:8],
                                            in_=st_a[:, 0:8])),
               deps=[b_xq], tag="xqs")
    o_xbe = emit_bcast_from_stats(None, None, [o_xqs, prev_amax], "xq")

    xq = hslab
    xq_ops = []
    for ch in range(CH):
        q1 = OP("V", (lambda v, ch=ch: v.tensor_tensor(
            out=tmp_v[ch % 2], in0=hslab[:, ch, :], in1=cinv_v[:],
            op=ALU.mult)), deps=[o_xbe, xq_ops[-2] if len(xq_ops) >= 2 else None],
            tag=f"xq1_{ch}")
        qb1 = OP("S", (lambda s: None), deps=[q1], tag=f"xqb1_{ch}")
        q2 = OP("V", (lambda v, ch=ch: v.tensor_scalar(
            out=tmp_v[ch % 2], in0=tmp_v[ch % 2], scalar1=RHE_C,
            scalar2=RHE_C, op0=ALU.add, op1=ALU.subtract)), deps=[qb1],
            tag=f"xq2_{ch}")
        qb2 = OP("S", (lambda s: None), deps=[q2], tag=f"xqb2_{ch}")
        q3 = OP("V", (lambda v, ch=ch: v.tensor_tensor(
            out=xq[:, ch, :], in0=tmp_v[ch % 2], in1=cs_v[:],
            op=ALU.mult)), deps=[qb2], tag=f"xq3_{ch}")
        xq_ops.append(q3)
    xq_last = xq_ops[-1]

    if os.environ.get("KMK_DBG") == "ffn7":
        DMA("G", dbg1[:, 0:256], cinv_v[:, 256:512], deps=[o_xbe],
            tag="dbgcv", slot="xs0")
        DMA("G", dbg2.bitcast(F32)[:, 0:8], st_a[:, 0:8], deps=[o_xbe],
            tag="dbgsta", slot="xs1")
        DMA("G", dbg3.bitcast(F32)[:, 0:8], st_b[:, 0:8], deps=[o_xbe],
            tag="dbgstb", slot="xl0")
        stages["f7d"] = len(B.ops)
    if os.environ.get("KMK_DBG") == "ffn8":
        dbb = dbg1.bitcast(BF16)
        DMA("G", dbb[:, 0:128], hslab[:, 0, 256:384], deps=[xq_last],
            tag="dbgx8a", slot="xs0")
        DMA("G", dbb[:, 128:256], hslab[:, 15, 256:384], deps=[xq_last],
            tag="dbgx8b", slot="xs1")
        DMA("G", dbb[:, 256:384], hslab[:, 0, 384:512], deps=[xq_last],
            tag="dbgx8c", slot="xl0")
        DMA("G", dbb[:, 384:512], hslab[:, 0, 128:256], deps=[xq_last],
            tag="dbgx8d", slot="xl1")
        stages["f8d"] = len(B.ops)
    if os.environ.get("KMK_DBG") == "ffn1":
        DMA("G", dbg2, gup1_ag.ap()[0:128, 0:128], deps=[ag_gup1],
            tag="dbgg1", slot="xs0")
        DMA("G", dbg3, down1_ag.ap()[0:128, 0:128], deps=[ag_down1],
            tag="dbgd1", slot="xs1")
        DMA("G", dbg1.bitcast(BF16)[:, 0:256], hslab[:, 0, 0:256],
            deps=[xq_last], tag="dbgxq", slot="xl0")
        stages["f1d"] = len(B.ops)

    g1rhs = vb(A_VS, CH, 2, 512)
    sg1_t = vb(A_BB, 512)
    htok = vb(A_QK, 4, H1)
    hq_stage = vb(A_QK + 24576, 4, 128)
    GPS1 = {0: psA[:, 0:512], 1: psA[:, 512:1024]}
    UPS1 = {0: psA[:, 1024:1536], 1: psB[:, 0:512]}

    prev_half_done = None
    hq_st_last = None
    for half in range(2):
        half_mm = {}
        for j in range(11):
            def rldfn(y, j=j):
                for k in range(CH):
                    y.dma_start(out=g1rhs[:, k, 0, :],
                                in_=gup1_ag.ap()[512 * j:512 * j + 512,
                                                 128 * k:128 * k + 128],
                                transpose=True).then_inc(sems["g1w"].h, 16)
                    y.dma_start(out=g1rhs[:, k, 1, :],
                                in_=gup1_ag.ap()[H1 + 512 * j:H1 + 512 * j + 512,
                                                 128 * k:128 * k + 128],
                                transpose=True).then_inc(sems["g1w"].h, 16)
                return None
            drl = OP("Y", rldfn, deps=[ag_gup1, xq_last, half_mm.get(j - 1),
                                       prev_half_done], tag=f"g1w{half}_{j}",
                     inc=16 * 32, self_inc=True, sem="g1w")
            hm_by_t4 = {}
            for t4 in range(4):
                tbg = 4 * half + t4
                par = t4 % 2

                def mmfn(t, j=j, t4=t4, tbg=tbg, par=par):
                    for k in range(CH):
                        t.matmul(GPS1[par], xq[:, k, 128 * tbg:128 * tbg + 128],
                                 g1rhs[:, k, 0, :], start=(k == 0), stop=(k == CH - 1))
                        t.matmul(UPS1[par], xq[:, k, 128 * tbg:128 * tbg + 128],
                                 g1rhs[:, k, 1, :], start=(k == 0), stop=(k == CH - 1))
                    return None
                mm = OP("T", mmfn, deps=[drl, xq_last, half_mm.get(j - 1),
                                         hm_by_t4.get(t4 - 2)],
                        tag=f"g1m{half}_{j}_{t4}")
                sil = OP("S", (lambda s, par=par: s.activation(
                    out=sg1_t[:], in_=GPS1[par], func=AF.Silu)), deps=[mm],
                    tag=f"g1s{half}_{j}_{t4}")
                hm = OP("V", (lambda v, par=par, j=j, t4=t4: v.tensor_tensor(
                    out=htok[:, t4, 512 * j:512 * j + 512], in0=sg1_t[:],
                    in1=UPS1[par], op=ALU.mult)), deps=[sil, mm],
                    tag=f"g1h{half}_{j}_{t4}")
                hm_by_t4[t4] = hm
                half_mm[j] = hm
        q_ops = []
        for t4 in range(4):
            am1 = OP("V", (lambda v, t4=t4: v.tensor_reduce(
                out=st_a[:, t4:t4 + 1].rearrange("p g -> p g ()"),
                in_=htok[:, t4, :].rearrange("p n -> p () n"),
                axis=AX.X, op=ALU.max, apply_absolute_value=True)),
                deps=[half_mm[10]], tag=f"g1a1{half}_{t4}")
            ab1 = OP("S", (lambda s: None), deps=[am1], tag=f"g1ab1{half}_{t4}")
            am2 = OP("V", (lambda v, t4=t4: v.tensor_scalar(
                out=st_a[:, t4:t4 + 1], in0=st_a[:, t4:t4 + 1],
                scalar1=1.0 / 127.0, scalar2=1e-8,
                op0=ALU.mult, op1=ALU.max)), deps=[ab1], tag=f"g1a2{half}_{t4}")
            ab2 = OP("S", (lambda s: None), deps=[am2], tag=f"g1ab2{half}_{t4}")
            am = OP("V", (lambda v, t4=t4: v.reciprocal(
                out=st_b[:, t4:t4 + 1], in_=st_a[:, t4:t4 + 1])),
                deps=[ab2], tag=f"g1a{half}_{t4}")
            ab3 = OP("S", (lambda s: None), deps=[am], tag=f"g1ab3{half}_{t4}")

            qq1 = OP("V", (lambda v, t4=t4: v.tensor_scalar(
                out=wf_r[:, 0:H1], in0=htok[:, t4, :], scalar1=st_b[:, t4:t4 + 1],
                scalar2=st_b[:, t4:t4 + 1], op0=ALU.mult, op1=ALU.bypass)),
                deps=[ab3], tag=f"g1q1{half}_{t4}")
            qb1 = OP("S", (lambda s: None), deps=[qq1], tag=f"g1qb1{half}_{t4}")
            qq2 = OP("V", (lambda v: v.tensor_scalar(
                out=wf_r[:, 0:H1], in0=wf_r[:, 0:H1], scalar1=RHE_C, scalar2=RHE_C,
                op0=ALU.add, op1=ALU.subtract)), deps=[qb1],
                tag=f"g1q2{half}_{t4}")
            qb2 = OP("S", (lambda s: None), deps=[qq2], tag=f"g1qb2{half}_{t4}")
            qq = OP("V", (lambda v, t4=t4: v.tensor_scalar(
                out=htok[:, t4, :], in0=wf_r[:, 0:H1],
                scalar1=st_a[:, t4:t4 + 1], scalar2=st_a[:, t4:t4 + 1],
                op0=ALU.mult, op1=ALU.bypass)), deps=[qb2],
                tag=f"g1q{half}_{t4}")
            q_ops.append(qq)

        if half == 0 and os.environ.get("KMK_DBG") == "ffn2":
            DMA("G", dbg2, htok[:, 0, 0:128], deps=[half_mm[10]],
                tag="dbgh0", slot="xs0")
            DMA("G", dbg3, htok[:, 0, 0:128], deps=[q_ops[0]],
                tag="dbghq", slot="xs1")
            stages["f2d"] = len(B.ops)
        if half == 0 and os.environ.get("KMK_DBG") == "ffn6":
            DMA("G", dbg2, htok[:, 2, 0:128], deps=[half_mm[10]],
                tag="dbgh2", slot="xs0")
            DMA("G", dbg3, htok[:, 2, 0:128], deps=[q_ops[2]],
                tag="dbghq2", slot="xs1")

        for t4 in range(4):
            tbg = 4 * half + t4
            for ic4 in range(11):
                def trfn(t, t4=t4, ic4=ic4):
                    for q in range(4):
                        ic = 4 * ic4 + q
                        t.transpose(psB[:, 512 * (ic4 % 2) + 64 * q:512 * (ic4 % 2) + 64 * q + 64]
                                    .bitcast(BF16),
                                    htok[:, t4, 128 * ic:128 * ic + 128], ident_b[:])
                    return None
                o_tr = OP("T", trfn, deps=[q_ops[t4], hq_st_last], tag=f"ht{half}_{t4}_{ic4}")
                o_ev = OP("V", (lambda v, ic4=ic4: v.tensor_copy(
                    out=hq_stage[:, :, :].rearrange("p a b -> p (a b)"),
                    in_=psB[:, 512 * (ic4 % 2):512 * (ic4 % 2) + 256].bitcast(BF16))),
                    deps=[o_tr], tag=f"he{half}_{t4}_{ic4}")
                hq_st = DMA("G", hq_scr[512 * ic4:512 * ic4 + 512,
                                        128 * tbg:128 * tbg + 128]
                            .rearrange("(a p) n -> p a n", p=128),
                            hq_stage[:, :, :], deps=[o_ev],
                            tag=f"hs{half}_{t4}_{ic4}", slot="hq0")
                hq_st_last = hq_st
        prev_half_done = hq_st_last

    if os.environ.get("KMK_DBG") == "ffn3":
        DMA("G", dbg1.bitcast(BF16)[:, 0:256], hq_scr[0:128, 0:256]
            .rearrange("p n -> p n"), deps=[prev_half_done], tag="dbghs",
            slot="xs0")
        stages["f3d"] = len(B.ops)
    if os.environ.get("KMK_DBG") == "ffn6":
        DMA("G", dbg1.bitcast(BF16)[:, 0:256], hq_scr[0:128, 256:512],
            deps=[prev_half_done], tag="dbghs6", slot="xl0")
        stages["f6d"] = len(B.ops)

    res4 = []
    prev_res = None
    dbg_ops4 = []
    for p4 in range(4):
        mm_last = None
        mms = {}
        for k in range(44):
            dwl = DMA("Y", pj_w[k % 2][:],
                      down1_ag.ap()[512 * p4:512 * p4 + 512, 128 * k:128 * k + 128],
                      deps=[ag_down1, hq_st_last, prev_res, mms.get(k - 2)]
                      + (dbg_ops4 if k == 2 else []),
                      tag=f"d1w{p4}_{k}", transpose=True, slot=f"pw{k % 2}")
            dhl = DMA("Y", pj_h[k % 2][:], hq_scr[128 * k:128 * k + 128, :],
                      deps=[hq_st_last, prev_res, mms.get(k - 2)]
                      + (dbg_ops4 if k == 2 else []),
                      tag=f"d1h{p4}_{k}", slot=f"ph{k % 2}")

            def mmfn(t, k=k):
                for mb in range(4):
                    for nb in range(2):
                        t.matmul(ACC[2 * mb + nb], pj_w[k % 2][:, 128 * mb:128 * mb + 128],
                                 pj_h[k % 2][:, 512 * nb:512 * nb + 512],
                                 start=(k == 0), stop=(k == 43))
                return None
            mm_last = OP("T", mmfn, deps=[dwl, dhl, prev_res], tag=f"d1m{p4}_{k}")
            mms[k] = mm_last
            if p4 == 0 and k == 0 and os.environ.get("KMK_DBG") == "ffn4":
                dbg_ops4.append(DMA("G", dbg2, pj_w[0][:, 0:128], deps=[mms[0]],
                                    tag="dbgpw", slot="xs0"))
                dbg_ops4.append(DMA("G", dbg3, pj_h[0][:, 0:128], deps=[mms[0]],
                                    tag="dbgph", slot="xs1"))
        if p4 == 0 and os.environ.get("KMK_DBG") == "ffn4":
            dacc = OP("S", (lambda s: s.activation(
                out=vf(F_WRK, 128)[:, :], in_=ACC[0][:, 0:128], func=AF.Copy)),
                deps=[mm_last], tag="dbgac")
            DMA("G", dbg1[:, 0:128], vf(F_WRK, 128)[:, :], deps=[dacc],
                tag="dbgacd", slot="xl0")
            stages["f4d"] = len(B.ops)
        groups = [(ACC[2 * mb], ACC[2 * mb + 1], 4 * p4 + mb) for mb in range(4)]
        sts = emit_residual(groups, mod1, 80, False, [mm_last], f"d1r{p4}",
                            to_xout=True)
        res4.extend(sts)
        prev_res = sts[-1]
        if p4 == 0 and os.environ.get("KMK_DBG") == "ffn5":
            DMA("G", dbg1[:, 0:256], xout[0:128, 0:256], deps=[sts[0]],
                tag="dbgxo", slot="xs0")
            stages["f5d"] = len(B.ops)

    OP("Y", lambda y: y.nop(), deps=res4, tag="final")

    if stop_stage is not None:
        n = stages[stop_stage]
        B.ops = B.ops[:n]
        last = B.ops[-1]
        dfin = DMA("G", xout, x_dram, deps=[last], tag="dbg_out", slot="xs0")
        OP("Y", lambda y: y.nop(), deps=[dfin] + [o for o in B.ops[:n]
                                                 if o.eng in ("C",)][-1:], tag="final_dbg")

    with allow_ici_all_to_all():
        B.finalize()
        _alloc_sems()
        blk = es.enter_context(nc.Block())
        B.emit(blk, sems)
    es.close()
    return nc


def _rotary_tables():
    freq = (1.0 / 1024.0) ** np.linspace(0.0, 1.0, HD // 4, dtype=np.float32)
    freq = np.concatenate([freq, np.zeros(HD // 4, dtype=np.float32)])
    t = np.arange(T, dtype=np.float32)
    theta = t[:, None] * freq[None, :].astype(np.float32)
    cos = np.cos(theta.astype(np.float32)).astype(np.float32)
    sin = np.sin(theta.astype(np.float32)).astype(np.float32)
    A = np.concatenate([cos.T, cos.T], axis=0)
    Bm = np.concatenate([sin.T, -sin.T], axis=0)
    return np.ascontiguousarray(A), np.ascontiguousarray(Bm)


_CACHE = {}


def kernel(L0_attn_x, L0_attn_Wq, L0_attn_Wk, L0_attn_Wv, L0_attn_Wo,
           L0_attn_rms_w, L0_ffn_w_gate_up, L0_ffn_w_proj, L0_c, L0_wmod,
           L1_attn_Wq, L1_attn_Wk, L1_attn_Wv, L1_attn_Wo,
           L1_ffn_w_gate_up, L1_ffn_w_down, L1_c, L1_wmod):
    f = lambda a: np.asarray(a, dtype=np.float32)
    xT = np.ascontiguousarray(f(L0_attn_x)[0].T)
    wq1T = np.ascontiguousarray(f(L1_attn_Wq).T)
    wk1T = np.ascontiguousarray(f(L1_attn_Wk).T)
    wv1T = np.ascontiguousarray(f(L1_attn_Wv).T)
    wo1T = np.ascontiguousarray(f(L1_attn_Wo).T)
    rotA, rotB = _rotary_tables()
    slopes = build_slopes(NH)
    i = np.arange(128, dtype=np.float64)
    ident = np.eye(128, dtype=np.float32)
    identb = np.eye(128, dtype=BFNP)
    perm = np.zeros((128, 128), dtype=BFNP)
    for m in range(128):
        perm[(m + 64) % 128, m] = 1.0
    caus = (np.arange(128)[:, None] <= np.arange(128)[None, :]).astype(np.float32)

    in_maps = []
    for c in range(NCORE):
        heads = [HPC * c + h for h in range(HPC)]
        r_h = np.exp(-slopes[heads])
        qdec = np.stack([(r_h[h] ** (i + 1.0)).astype(np.float32)
                         for h in range(HPC)], axis=1)
        kdec = np.stack([(r_h[h] ** (127.0 - i)).astype(np.float32)
                         for h in range(HPC)], axis=1)
        rcc = np.stack([np.full(128, r_h[h] ** 128.0, dtype=np.float32)
                        for h in range(HPC)], axis=1)
        dm = []
        for h in range(HPC):
            diff = i[:, None] - i[None, :]
            mask = diff >= 0
            d = np.where(mask, r_h[h] ** np.where(mask, diff, 0.0), 0.0)
            dm.append(d.T.astype(np.float32))
        dmask = np.ascontiguousarray(np.concatenate(dm, axis=1))

        m = {
            "xin": np.ascontiguousarray(xT[:, TC * c:TC * c + TC]),
            "wq0s": np.ascontiguousarray(f(L0_attn_Wq)[:, 256 * c:256 * c + 256]),
            "wk0s": np.ascontiguousarray(f(L0_attn_Wk)[:, 256 * c:256 * c + 256]),
            "wv0s": np.ascontiguousarray(f(L0_attn_Wv)[:, 256 * c:256 * c + 256]),
            "wo0": np.ascontiguousarray(f(L0_attn_Wo)),
            "wq1s": np.ascontiguousarray(wq1T[:, 256 * c:256 * c + 256]),
            "wk1s": np.ascontiguousarray(wk1T[:, 256 * c:256 * c + 256]),
            "wv1s": np.ascontiguousarray(wv1T[:, 256 * c:256 * c + 256]),
            "wo1t": wo1T,
            "rmspp": np.ascontiguousarray(f(L0_attn_rms_w).reshape(CH, 128).T),
            "wmod0s": np.ascontiguousarray(f(L0_wmod)[:, 1536 * c:1536 * c + 1536]),
            "wmod1s": np.ascontiguousarray(f(L1_wmod)[:, 1536 * c:1536 * c + 1536]),
            "c0r": np.ascontiguousarray(f(L0_c).reshape(CH, 128)),
            "c1r": np.ascontiguousarray(f(L1_c).reshape(CH, 128)),
            "gup0s": np.ascontiguousarray(f(L0_ffn_w_gate_up)[2048 * c:2048 * c + 2048]),
            "proj0s": np.ascontiguousarray(f(L0_ffn_w_proj)[256 * c:256 * c + 256]),
            "gup1s": np.ascontiguousarray(f(L1_ffn_w_gate_up)[1408 * c:1408 * c + 1408]),
            "down1s": np.ascontiguousarray(f(L1_ffn_w_down)[256 * c:256 * c + 256]),
            "rotA": rotA, "rotB": rotB,
            "qdecin": qdec, "kdecin": kdec, "rcin": rcc,
            "dmaskin": dmask, "causin": caus,
            "identfin": ident, "identbin": identb, "permin": perm,
        }
        in_maps.append(m)

    stop = os.environ.get("KMK_STOP") or None
    if "nc" not in _CACHE or _CACHE.get("stage") != stop:
        _CACHE["nc"] = build_program(stop)
        _CACHE["stage"] = stop
    nc = _CACHE["nc"]
    with allow_ici_all_to_all():
        res = run_bass_kernel_spmd(nc, in_maps, core_ids=list(range(NCORE)))
    slabs = [np.asarray(res.results[c]["xout"]) for c in range(NCORE)]
    out = np.concatenate(slabs, axis=1)
    return np.ascontiguousarray(out.T)[None].astype(np.float32)
